# version 11 (final)
# baseline (speedup 1.0000x reference)
import os as _os
import numpy as np
import ml_dtypes
import concourse.bass as bass
import concourse.mybir as mybir
from concourse.bass_utils import run_bass_kernel_spmd

F32 = mybir.dt.float32
BF16 = mybir.dt.bfloat16
AL = mybir.AluOpType
AF = mybir.ActivationFunctionType

D, CH = 1024, 8
H, FD, DH = 16, 16, 64
E, HE, HCN, HS = 16, 512, 4, 1024
EGRP, NEG = 8, 2
RS, NSLOT = 2048, 5


HZ_RATE = 0.35
HZ_LAG = 600.0


class B:
    def __init__(self):
        self.ops = []
        self.reg = {}
        self.ndma = {"gp": 0, "sp": 0}

    def emit(self, eng, fn, reads=(), writes=(), dma=False, tag="", cols=0):
        deps = set()
        raw = set()
        for r in reads:
            st = self.reg.get(r)
            if st:
                if st["w"] is not None:
                    deps.add(st["w"])
                    raw.add(st["w"])
                deps.update(st.get("x", ()))
        for w in writes:
            st = self.reg.get(w)
            if st:
                if st["w"] is not None:
                    deps.add(st["w"])
                deps.update(st["r"])
                deps.update(st.get("x", ()))
        idx = len(self.ops)
        op = dict(eng=eng, fn=fn, dma=dma, deps=deps, raw=raw, tag=tag,
                  awaited=False, cols=cols)
        if dma:
            op["dseq"] = self.ndma[eng]
            self.ndma[eng] += 1
        self.ops.append(op)
        for r in reads:
            self.reg.setdefault(r, {"w": None, "r": set()})["r"].add(idx)
        for w in writes:
            self.reg[w] = {"w": idx, "r": set()}
        return idx

    def barrier_into(self, new_regions, old_prefixes):
        evts = set()
        for k, st in self.reg.items():
            k0 = k[0] if isinstance(k, tuple) else k
            if k0 in old_prefixes:
                if st["w"] is not None:
                    evts.add(st["w"])
                evts.update(st["r"])
        if not evts:
            return
        for rn in new_regions:
            st = self.reg.setdefault(rn, {"w": None, "r": set()})
            st["x"] = set(st.get("x", ())) | evts

    def finalize(self):
        epos, elists = {}, {}
        for idx, op in enumerate(self.ops):
            if op["dma"]:
                continue
            lst = elists.setdefault(op["eng"], [])
            epos[idx] = len(lst)
            lst.append(idx)
        cum = {}
        for e, lst in elists.items():
            c = [0.0]
            for idx in lst:
                c.append(c[-1] + HZ_RATE * self.ops[idx]["cols"])
            cum[e] = c
        flushed = {e: -1 for e in elists}
        for e, lst in elists.items():
            for jp, idx in enumerate(lst):
                op = self.ops[idx]
                for d in op["raw"]:
                    dop = self.ops[d]
                    if dop["dma"] or dop["eng"] != e:
                        continue
                    dp = epos[d]
                    if dp <= flushed[e]:
                        continue
                    if cum[e][jp] - cum[e][dp] < HZ_LAG:
                        op["drainb"] = True
                        flushed[e] = jp - 1
                        break
        for op in self.ops:
            for d in op["deps"]:
                dop = self.ops[d]
                if dop["dma"] or dop["eng"] != op["eng"]:
                    dop["awaited"] = True
        cnt = {}
        for op in self.ops:
            if op["dma"]:
                op["ev"] = 16 * (op["dseq"] + 1)
            elif op["awaited"]:
                cnt[op["eng"]] = cnt.get(op["eng"], 0) + 1
                op["ev"] = cnt[op["eng"]]
            else:
                op["ev"] = None
        for op in self.ops:
            w = {}
            for d in op["deps"]:
                dop = self.ops[d]
                if not dop["dma"] and dop["eng"] == op["eng"]:
                    continue
                key = (dop["eng"] + "dma") if dop["dma"] else dop["eng"]
                if dop["ev"] is not None:
                    w[key] = max(w.get(key, 0), dop["ev"])
            op["waits"] = w


def _pad_qk(w):
    out = np.zeros((D, 512), np.float32)
    for h in range(H):
        c0 = 128 * (h // 4) + 32 * (h % 4)
        out[:, c0:c0 + FD] = w[:, h * FD:(h + 1) * FD]
    return out


def _lhsT_tiles(w, mm=128):
    Kd, M = w.shape
    nk, nm = Kd // 128, M // mm
    a = w.reshape(nk, 128, nm, mm).transpose(2, 0, 1, 3)
    return np.ascontiguousarray(a.transpose(2, 0, 1, 3).reshape(128, nm * nk * mm))


def build_wblob(ws):
    parts, offs, cur = [], {}, [0]

    def add(name, a):
        a = np.ascontiguousarray(a, np.float32)
        if a.shape[0] != 128:
            a = np.concatenate([a, np.zeros((128 - a.shape[0], a.shape[1]),
                                            np.float32)], 0)
        offs[name] = (cur[0], a.shape[1])
        parts.append(a)
        cur[0] += a.shape[1]

    add("wmod", _lhsT_tiles(ws["wmod"], 512))
    add("wq", _lhsT_tiles(_pad_qk(ws["Wq"])))
    add("wk", _lhsT_tiles(_pad_qk(ws["Wk"])))
    ffn0 = cur[0]
    for g in range(NEG):
        for e in range(g * EGRP, (g + 1) * EGRP):
            wt = ws["e_win"][e].reshape(CH, 128, HCN, 128)
            vt2 = ws["e_vin"][e].reshape(CH, 128, HCN, 128)
            segs = []
            for hc in range(HCN):
                segs.append(wt[:, :, hc, :].transpose(1, 0, 2).reshape(128, CH * 128))
                segs.append(vt2[:, :, hc, :].transpose(1, 0, 2).reshape(128, CH * 128))
            add(("wv_e", e), np.concatenate(segs, 1))
        if g == 0:
            wsi = ws["ws_in"].reshape(CH, 128, CH, 128)
            vsi = ws["vs_in"].reshape(CH, 128, CH, 128)
            for m in range(CH):
                a = wsi[:, :, m, :].transpose(1, 0, 2).reshape(128, CH * 128)
                b2 = vsi[:, :, m, :].transpose(1, 0, 2).reshape(128, CH * 128)
                add(("wsh", m), np.concatenate([a, b2], 1))
        wso = ws["ws_out"].reshape(CH, 128, CH, 128)
        ewo = ws["e_wout"].reshape(E, HCN, 128, CH, 128)
        for m in range(CH):
            blobs = []
            if g == 0:
                blobs.append(wso[:, :, m, :].transpose(1, 0, 2).reshape(128, CH * 128))
            for e in range(g * EGRP, (g + 1) * EGRP):
                blobs.append(ewo[e, :, :, m, :].transpose(1, 0, 2).reshape(128, HCN * 128))
            add(("wout", g, m), np.concatenate(blobs, 1))
    ffn_len = cur[0] - ffn0
    assert ffn0 % RS == 0 and ffn_len % RS == 0, (ffn0, ffn_len)
    add("wv", ws["Wv"].reshape(CH, 128, D).transpose(1, 0, 2).reshape(128, CH * D))
    add("gate", np.concatenate(
        [ws["gate_w"].T.reshape(CH, 128, E).transpose(1, 0, 2).reshape(128, CH * E),
         np.zeros((128, 2048 - CH * E), np.float32)], 1))
    add("wo", ws["Wo"].reshape(CH, 128, CH, 128).transpose(1, 0, 2, 3).reshape(128, CH * D))
    return np.concatenate(parts, 1), offs, ffn0, ffn_len


def build_program(S, wblob_cols, woffs, ffn0, ffn_len):
    NSB = S // 512
    NTC = S // 128
    nc = bass.Bass("TRN2", target_bir_lowering=False, debug=False,
                   detect_race_conditions=False)
    bd = B()

    x_in = nc.dram_tensor("x_in", [D, S], F32, kind="ExternalInput").ap()
    wb = nc.dram_tensor("wb", [128, wblob_cols], F32, kind="ExternalInput").ap()
    c_in = nc.dram_tensor("c_in", [128, CH], F32, kind="ExternalInput").ap()
    tri_f_in = nc.dram_tensor("tri_f", [128, 128], F32, kind="ExternalInput").ap()
    tri_b_in = nc.dram_tensor("tri_b", [128, 128], BF16, kind="ExternalInput").ap()
    id_b_in = nc.dram_tensor("id_b", [128, 128], BF16, kind="ExternalInput").ap()
    id16_in = nc.dram_tensor("id16_b", [16, 16], BF16, kind="ExternalInput").ap()
    id16f_in = nc.dram_tensor("id16_f", [16, 16], F32, kind="ExternalInput").ap()
    sel_in = nc.dram_tensor("sel_b", [16, E * 128], BF16, kind="ExternalInput").ap()
    aux_in = nc.dram_tensor("aux_f", [128, 4], F32, kind="ExternalInput").ap()
    x_out = nc.dram_tensor("x_out", [D, S], F32, kind="ExternalOutput").ap()
    _dbg_big = bool(_os.environ.get("KDBG_HIN")) or _os.environ.get("KSTOP") == "mod"
    dbg_shape = [128, 2, 512] if _dbg_big else [128, 4]
    dbg_out = nc.dram_tensor("dbg_out", dbg_shape, F32, kind="ExternalOutput").ap()

    A = nc.alloc_sbuf_tensor
    xs = A("xs", [128, CH, S], F32).ap()
    ring = A("ring", [128, NSLOT * RS], BF16).ap()
    atn_sz = max(CH * S * 2, 41216)
    qk_sz = max(2 * 4 * S * 2, EGRP * HCN * 512 * 2)
    v_sz = max(NTC * H * 65 * 2, 16384 + 8192,
               12288 + 6 * S + 288 + 2048 + 4096)
    atnT = A("atnA", [128, atn_sz], mybir.dt.uint8)
    qkT = A("qkA", [128, qk_sz], mybir.dt.uint8)
    vT = A("vA", [128, v_sz], mybir.dt.uint8)
    wres = A("wres", [128, CH, D], BF16).ap()
    gatesb = A("gatesb", [128, CH, E], BF16).ap()
    dbgs = A("dbgs", dbg_shape, F32).ap()
    csb = A("csb", [128, CH], F32).ap()
    siluc = A("siluc", [128, CH], BF16).ap()
    modv = A("modv", [128, 6, CH], F32).ap()
    modt = A("modt", [128, CH], F32).ap()
    modt2 = A("modt2", [128, CH], F32).ap()
    modt3 = A("modt3", [128, CH], F32).ap()
    diag_t = A("diag_t", [128, 4, 128], BF16).ap()
    trif = A("trif", [128, 128], F32).ap()
    trib = A("trib", [128, 128], BF16).ap()
    idb = A("idb", [128, 128], BF16).ap()
    id16s = A("id16", [16, 16], BF16).ap()
    id16f = A("id16f", [16, 16], F32).ap()
    auxs = A("auxs", [128, 4], F32).ap()

    ONESROW = lambda n=128: trib[0:1, 0:n]
    ONESCOL = trib[:, 127:128]
    ONE_B = trib[0:1, 127:128]
    ONE_F = trif[0:1, 127:128]

    def ov(name, shape, dtype, backing, off):
        h = nc.virtual_sbuf_tensor(name, shape, dtype)
        nc._tensor(h.name, h.shape, h.dtype, type="SB")
        m = nc.lookup_mloc(h)
        m.allocated = True
        m.addr = nc.lookup_mloc(backing).addr + off
        return h.ap()

    hmod = ov("hmod", [128, CH, S], BF16, atnT, 0)
    y_st = ov("y_st", [128, CH, 512], BF16, atnT, 0)
    Asb = ov("Asb", [128, 4, 512], BF16, atnT, 8192)
    dvrf = ov("dvrf", [1, 4, 512], F32, atnT, 12288)
    dvrb = ov("dvrb", [1, 4, 512], BF16, atnT, 20480)
    cumc = ov("cumc", [1, 2, 1040], F32, atnT, 24576)
    cumb = ov("cumb", [1, 4, 1040], BF16, atnT, 32896)
    Q = ov("Qfm", [128, 4, S], BF16, qkT, 0)
    K = ov("Kfm", [128, 4, S], BF16, qkT, 4 * S * 2)
    h_grp = ov("h_grp", [128, EGRP, HCN, 512], BF16, qkT, 0)
    Vp = ov("Vp", [128, NTC, H, 65], BF16, vT, 0)
    modrow = ov("modrow", [1, 512], F32, vT, 0)
    stat = ov("stat", [1, 2, 512], F32, vT, 2048)
    statb = ov("statb", [1, 2, 512], BF16, vT, 6144)
    tmprow = ov("tmprow", [1, 2, 512], F32, vT, 8192)
    xbuf = ov("xbuf", [128, 4, 512], BF16, vT, 12288)
    tmpa = ov("tmpa", [128, 4, 512], F32, vT, 16384)
    sel = ov("sel", [16, E * 128], BF16, vT, 0)
    s_in = ov("s_in", [128, CH, 512], BF16, vT, 4096)
    g_s = ov("g_s", [16, S], F32, vT, 12288)
    comb = ov("comb", [16, S], BF16, vT, 12288 + 4 * S)
    gt4 = ov("gt4", [128, 4, 16], F32, vT, 31008)
    m84 = ov("m84", [128, 4, 8], F32, vT, 31264)
    zap4 = ov("zap4", [128, 4, 16], F32, vT, 31392)
    mg4 = ov("mg4", [128, 4, 16], F32, vT, 31648)
    s44 = ov("s44", [128, 4, 2], F32, vT, 31904)
    ctm4 = ov("ctm4", [128, 4, 16], BF16, vT, 31936)
    cbc = ov("cbc", [128, 2, 512], BF16, vT, 12288 + 6 * S + 288)
    fbuf = ov("fbuf", [128, 4, 512], BF16, vT, 12288 + 6 * S + 288 + 2048)

    psA = nc.alloc_psum_tensor("psA", [128, 4, 512], F32).ap()
    psB = nc.alloc_psum_tensor("psB", [128, 2, 512], F32).ap()
    psC = nc.alloc_psum_tensor("psC", [128, 2, 512], F32).ap()

    sem = {n: nc.alloc_semaphore(name="s_" + n)
           for n in ("pe", "act", "dve", "sp", "gp", "gpdma", "spdma")}

    nc.const_aps.register(F32, 1e-5, auxs[:, 0:1])
    nc.const_aps.register(F32, 4.0, auxs[:, 2:3])
    nc.const_aps.register(F32, 1e-9, auxs[:, 3:4])
    nc.const_aps.register(F32, -1.0, auxs[:, 1:2])

    RA = lambda j: ("ps", j)
    RB = lambda i: ("ps", 4 + i)
    RC = lambda i: ("ps", 6 + i)
    HM = [("hmod", k, nt) for k in range(CH) for nt in range(NSB)]
    P4SCR = ("y_st", "Asb", "dvrf", "dvrb", "cumc", "cumb")
    LNSCR = ("modrow", "stat", "statb", "tmprow", "xbuf", "tmpa")

    n_head_slots = ffn0 // RS
    n_ffn_slots = ffn_len // RS
    st8 = dict(cursor=0, emitted=-1)

    def vslot_src(v):
        if v < n_head_slots:
            return v * RS
        return ffn0 + ((v - n_head_slots) % n_ffn_slots) * RS

    def wchunk(cols):
        g0 = st8["cursor"]
        assert g0 // RS == (g0 + cols - 1) // RS, (g0, cols)
        v = g0 // RS
        while st8["emitted"] < v:
            nv = st8["emitted"] + 1
            src0 = vslot_src(nv)
            dst = ring[:, (nv % NSLOT) * RS:(nv % NSLOT) * RS + RS]
            src = wb[:, src0:src0 + RS]
            bd.emit("gp", (lambda g, d=dst, s=src: g.dma_start(out=d, in_=s)),
                    reads=(), writes=(("ring", nv % NSLOT),), dma=True,
                    tag=f"wslot{nv}")
            st8["emitted"] = nv
        st8["cursor"] = g0 + cols
        pos = (v % NSLOT) * RS + g0 % RS
        return ring[:, pos:pos + cols], ("ring", v % NSLOT)

    def dma_sp(dst, src, wr, tag):
        bd.emit("sp", (lambda e, d=dst, s=src: e.dma_start(out=d, in_=s)),
                reads=(), writes=wr, dma=True, tag=tag)

    dma_sp(csb, c_in, (("csb",),), "ld_c")
    dma_sp(trif, tri_f_in, (("trif",),), "ld_trif")
    dma_sp(trib, tri_b_in, (("trib",),), "ld_trib")
    dma_sp(idb, id_b_in, (("idb",),), "ld_idb")
    dma_sp(id16s, id16_in, (("id16",),), "ld_id16")
    dma_sp(id16f, id16f_in, (("id16f",),), "ld_id16f")
    dma_sp(auxs, aux_in, (("aux",),), "ld_aux")
    for ch in range(CH):
        dma_sp(xs[:, ch, :], x_in[ch * 128:(ch + 1) * 128, :],
               tuple(("xs", ch, nt) for nt in range(NSB)), f"ld_x{ch}")

    bd.emit("act", lambda e: e.activation(siluc, csb, AF.Silu),
            reads=(("csb",),), writes=(("siluc",),), tag="siluC", cols=8)
    for jt in range(12):
        for k in range(CH):
            chk, rr = wchunk(512)
            bd.emit("pe", (lambda t, c=chk, k=k: t.matmul(
                psC[0:1, 0, :], siluc[:, k:k + 1], c,
                start=(k == 0), stop=(k == 7))),
                reads=(rr, ("siluc",)), writes=(RC(0),), tag="mod_mm")
        bd.emit("dve", lambda e: e.tensor_copy(modrow[0:1, :], psC[0:1, 0, :]),
                reads=(RC(0),), writes=(("modrow",),), tag="mod_ev", cols=512)
        for q in range(4):
            bd.emit("pe", (lambda t, q=q: t.matmul(
                psC[:, 1, q:q + 1], modrow[0:1, q * 128:(q + 1) * 128], ONE_F,
                start=True, stop=True)),
                reads=(("modrow",), ("trif",)), writes=(RC(1),), tag="mod_T")
        for q in range(4):
            cg = jt * 512 + q * 128
            vec, ch = cg // 1024, (cg % 1024) // 128
            bd.emit("dve", (lambda e, vec=vec, ch=ch, q=q: e.tensor_copy(
                modv[:, vec, ch:ch + 1], psC[:, 1, q:q + 1])),
                reads=(RC(1),), writes=(("modv",),), tag="mod_Tev", cols=1)
    bd.emit("dve", lambda e: e.tensor_scalar(modv[:, 1, :], modv[:, 1, :], 1.0,
                                             None, op0=AL.add),
            reads=(("modv",),), writes=(("modv",),), tag="ca1", cols=8)
    bd.emit("dve", lambda e: e.tensor_scalar(modv[:, 4, :], modv[:, 4, :], 1.0,
                                             None, op0=AL.add),
            reads=(("modv",),), writes=(("modv",),), tag="cb1", cols=8)
    for vec in (2, 5):
        g_ap = modv[:, vec, :]
        bd.emit("dve", (lambda e, g=g_ap: e.tensor_scalar(
            modt, g, 0.0, None, op0=AL.is_ge)),
            reads=(("modv",),), writes=(("modt",),), tag="sgn0", cols=8)
        bd.emit("dve", lambda e: e.tensor_scalar(modt, modt, 2.0, -1.0,
                                                 op0=AL.mult, op1=AL.add),
                reads=(("modt",),), writes=(("modt",),), tag="sgn1", cols=8)
        bd.emit("dve", (lambda e, g=g_ap: e.tensor_mul(modt2, g, modt)),
                reads=(("modv",), ("modt",)), writes=(("modt2",),), tag="gabs", cols=8)
        bd.emit("dve", lambda e: e.tensor_scalar(
            modt3, modt2, 0.25, None, op0=AL.max),
            reads=(("modt2",),), writes=(("modt3",),), tag="clamp", cols=8)
        bd.emit("dve", (lambda e, g=g_ap: e.tensor_mul(g, modt3, modt)),
                reads=(("modt3",), ("modt",)), writes=(("modv",),), tag="gsign", cols=8)

    def layer_norm(savec, cavec, tagp):
        for nt in range(NSB):
            i = nt % 2
            for k in range(CH):
                xb_ = xbuf[:, 2 * (k % 2), :]
                xq_ = xbuf[:, 2 * (k % 2) + 1, :]
                bd.emit("act", (lambda e, k=k, nt=nt, x_=xb_: e.activation(
                    x_, xs[:, k, nt * 512:(nt + 1) * 512], AF.Copy)),
                    reads=(("xs", k, nt),), writes=(("xbuf", k % 2),),
                    tag=tagp + "xb", cols=512)
                bd.emit("dve", (lambda e, x_=xb_, q_=xq_: e.tensor_mul(q_, x_, x_)),
                        reads=(("xbuf", k % 2),), writes=(("xbq", k % 2),),
                        tag=tagp + "xq", cols=512)
                bd.emit("pe", (lambda t, k=k, x_=xb_, i=i: t.matmul(
                    psB[0:1, i, :], ONESCOL, x_, start=(k == 0), stop=(k == 7))),
                    reads=(("xbuf", k % 2), ("trib",)), writes=(RB(i),),
                    tag=tagp + "mmM")
                bd.emit("pe", (lambda t, k=k, q_=xq_, i=i: t.matmul(
                    psC[0:1, i, :], ONESCOL, q_, start=(k == 0), stop=(k == 7))),
                    reads=(("xbq", k % 2), ("trib",)), writes=(RC(i),),
                    tag=tagp + "mmE")
            bd.emit("dve", (lambda e, i=i: e.tensor_scalar(
                stat[0:1, 0, :], psB[0:1, i, :], 1.0 / D, None, op0=AL.mult)),
                reads=(RB(i),), writes=(("stat", 0),), tag=tagp + "m", cols=512)
            bd.emit("dve", (lambda e, i=i: e.tensor_scalar(
                stat[0:1, 1, :], psC[0:1, i, :], 1.0 / D, None, op0=AL.mult)),
                reads=(RC(i),), writes=(("stat", 1),), tag=tagp + "e2", cols=512)
            bd.emit("dve", lambda e: e.tensor_mul(tmprow[0:1, 0, :], stat[0:1, 0, :],
                                                  stat[0:1, 0, :]),
                    reads=(("stat", 0),), writes=(("tmprow",),), tag=tagp + "m2", cols=512)
            bd.emit("dve", lambda e: e.tensor_sub(tmprow[0:1, 1, :], stat[0:1, 1, :],
                                                  tmprow[0:1, 0, :]),
                    reads=(("stat", 1), ("tmprow",)), writes=(("tmprow",),),
                    tag=tagp + "var", cols=512)
            bd.emit("act", lambda e: e.activation(tmprow[0:1, 0, :], tmprow[0:1, 1, :],
                                                  AF.Sqrt, bias=1e-5),
                    reads=(("tmprow",), ("aux",)), writes=(("tmprow",),),
                    tag=tagp + "sd", cols=512)
            bd.emit("dve", lambda e: e.reciprocal(stat[0:1, 1, :],
                                                  tmprow[0:1, 0, :]),
                    reads=(("tmprow",),), writes=(("stat", 1),), tag=tagp + "rs", cols=512)
            bd.emit("dve", lambda e: e.tensor_copy(statb[0:1, 0, :], stat[0:1, 0, :]),
                    reads=(("stat", 0),), writes=(("statb",),), tag=tagp + "cm", cols=512)
            bd.emit("dve", lambda e: e.tensor_copy(statb[0:1, 1, :], stat[0:1, 1, :]),
                    reads=(("stat", 1),), writes=(("statb",),), tag=tagp + "cr", cols=512)
            bd.emit("pe", (lambda t, i=i: t.matmul(
                psB[:, i, :], ONESROW(), statb[0:1, 0, :], start=True, stop=True)),
                reads=(("statb",), ("trib",)), writes=(RB(i),), tag=tagp + "bm")
            bd.emit("pe", (lambda t, i=i: t.matmul(
                psC[:, i, :], ONESROW(), statb[0:1, 1, :], start=True, stop=True)),
                reads=(("statb",), ("trib",)), writes=(RC(i),), tag=tagp + "br")
            for k in range(CH):
                tp = tmpa[:, 2 * (k % 2), :]
                tq = tmpa[:, 2 * (k % 2) + 1, :]
                bd.emit("dve", (lambda e, k=k, nt=nt, tp=tp, i=i: e.tensor_sub(
                    tp, xs[:, k, nt * 512:(nt + 1) * 512], psB[:, i, :])),
                    reads=(("xs", k, nt), RB(i)), writes=(("tmpa", k % 2),),
                    tag=tagp + "sub", cols=512)
                bd.emit("dve", (lambda e, tp=tp, tq=tq, i=i: e.tensor_mul(
                    tq, tp, psC[:, i, :])),
                    reads=(("tmpa", k % 2), RC(i)), writes=(("tmpa", k % 2),),
                    tag=tagp + "mul", cols=512)
                bd.emit("dve", (lambda e, k=k, nt=nt, tq=tq, sv=savec, cv=cavec:
                                e.tensor_scalar(
                    hmod[:, k, nt * 512:(nt + 1) * 512], tq,
                    modv[:, cv, k:k + 1], modv[:, sv, k:k + 1],
                    op0=AL.mult, op1=AL.add)),
                    reads=(("tmpa", k % 2), ("modv",)), writes=(("hmod", k, nt),),
                    tag=tagp + "mod", cols=512)

    _stop = _os.environ.get("KSTOP", "")
    if _stop == "mod":
        bd.emit("dve", lambda e: e.tensor_copy(dbgs[:, 0, 0:48],
                modv.rearrange("p a b -> p (a b)")),
                reads=(("modv",),), writes=(("dbgs",),), tag="dbgmod")
        bd.emit("sp", lambda e: e.dma_start(out=dbg_out, in_=dbgs),
                reads=(("dbgs",),), writes=(), dma=True, tag="st_dbg")
        bd.finalize()
        _emit_all(nc, bd, sem)
        return nc
    bd.barrier_into(HM, P4SCR)
    layer_norm(0, 1, "ln1_")
    if _stop == "ln1":
        for ch in range(CH):
            bd.emit("gp", (lambda e, ch=ch: e.dma_start(
                out=x_out[ch * 128:(ch + 1) * 128, :], in_=hmod[:, ch, :])),
                reads=tuple(("hmod", ch, nt) for nt in range(NSB)), writes=(),
                dma=True, tag=f"sth{ch}")
        bd.emit("dve", lambda e: e.memset(dbgs, 0.0),
                reads=(), writes=(("dbgs",),), tag="dbg_z")
        bd.emit("sp", lambda e: e.dma_start(out=dbg_out, in_=dbgs),
                reads=(("dbgs",),), writes=(), dma=True, tag="st_dbg")
        bd.finalize()
        _emit_all(nc, bd, sem)
        return nc

    def proj_qk(dst, rtag):
        for m in range(4):
            for k in range(CH):
                chk, rr = wchunk(128)
                for nt in range(NSB):
                    bd.emit("pe", (lambda t, c=chk, nt=nt, k=k: t.matmul(
                        psA[:, nt, :], c, hmod[:, k, nt * 512:(nt + 1) * 512],
                        start=(k == 0), stop=(k == 7))),
                        reads=(rr, ("hmod", k, nt)), writes=(RA(nt),),
                        tag=rtag + "mm")
            for nt in range(NSB):
                bd.emit("act", (lambda e, m=m, nt=nt, d=dst: e.activation(
                    d[:, m, nt * 512:(nt + 1) * 512], psA[:, nt, :], AF.Copy)),
                    reads=(RA(nt),), writes=((rtag, m, nt),), tag=rtag + "ev", cols=512)

    proj_qk(Q, "q")
    proj_qk(K, "k")

    woff_wv = woffs["wv"][0]
    for i2 in range(4):
        bd.emit("gp", (lambda g, i2=i2: g.dma_start(
            out=wres[:, 2 * i2:2 * i2 + 2, :].rearrange("p a b -> p (a b)"),
            in_=wb[:, woff_wv + i2 * RS: woff_wv + (i2 + 1) * RS])),
            reads=(), writes=(("wres", i2),), dma=True, tag="ld_wv")
    woff_g = woffs["gate"][0]
    bd.emit("gp", (lambda g: g.dma_start(
        out=gatesb.rearrange("p a b -> p (a b)"),
        in_=wb[:, woff_g: woff_g + CH * E])),
        reads=(), writes=(("gatesb",),), dma=True, tag="ld_gate")

    bd.barrier_into([("Vp", tc) for tc in range(NTC)], LNSCR)
    bd.emit("dve", lambda e: e.memset(
        Vp.rearrange("p t h v -> p (t h) v")[:, :, 64:65], 1.0 / 32.0),
        reads=(), writes=tuple(("Vp", tc) for tc in range(NTC)), tag="v_ones", cols=256)
    for tc in range(NTC):
        nt = tc // 4
        for k in range(CH):
            bd.emit("pe", (lambda t, tc=tc, k=k: t.matmul(
                psB[:, 0, :], hmod[:, k, tc * 128:(tc + 1) * 128],
                wres[:, k, 0:512], start=(k == 0), stop=(k == 7))),
                reads=(("hmod", k, nt), ("wres", k // 2)), writes=(RB(0),),
                tag="v_mmA")
            bd.emit("pe", (lambda t, tc=tc, k=k: t.matmul(
                psB[:, 1, :], hmod[:, k, tc * 128:(tc + 1) * 128],
                wres[:, k, 512:1024], start=(k == 0), stop=(k == 7))),
                reads=(("hmod", k, nt), ("wres", k // 2)), writes=(RB(1),),
                tag="v_mmB")
        for half in range(2):
            bd.emit("dve", (lambda e, tc=tc, half=half: e.tensor_scalar(
                Vp[:, tc, half * 8:(half + 1) * 8, 0:64],
                psB[:, half, :].rearrange("p (h v) -> p h v", v=64),
                1.0 / 32.0, None, op0=AL.mult)),
                reads=(RB(half),), writes=(("Vp", tc),), tag="v_ev", cols=512)

    woff_wo = woffs["wo"][0]
    for i2 in range(4):
        bd.emit("gp", (lambda g, i2=i2: g.dma_start(
            out=wres[:, 2 * i2:2 * i2 + 2, :].rearrange("p a b -> p (a b)"),
            in_=wb[:, woff_wo + i2 * RS: woff_wo + (i2 + 1) * RS])),
            reads=(("wres", i2),), writes=(("wres", i2),), dma=True, tag="ld_wo")

    bd.barrier_into([(p,) for p in ("y_st", "dvrf", "dvrb", "cumc")]
                    + [("Asb", hl) for hl in range(4)]
                    + [("y_st", h) for h in range(H)]
                    + [("dvrf", hl) for hl in range(4)]
                    + [("dvrb", hl) for hl in range(4)]
                    + [("cumb", j) for j in range(4)],
                    ("hmod",))
    bd.emit("dve", lambda e: e.memset(cumc[0:1, 0, :], 0.0),
            reads=(), writes=(("cumc",),), tag="cum0", cols=1040)
    UB = [psB[:, 0, :], psB[:, 1, :], psC[:, 0, :], psC[:, 1, :]]
    UBR = [RB(0), RB(1), RC(0), RC(1)]
    cpar = [0]

    def emit_strips(sb, hg, t):
        sclo = max(t, 4 * sb)
        soff = (sclo - 4 * sb) * 128
        nw = (4 * sb + 4 - sclo) * 128
        for hl in range(4):
            bp = hl * 32
            bd.emit("pe", (lambda t_, ub=UB[hl], bp=bp, hg=hg, tt=t,
                            soff=soff, nw=nw, sb=sb: t_.matmul(
                ub[:, soff:soff + nw],
                K[bp:bp + 32, hg, tt * 128:(tt + 1) * 128],
                Q[bp:bp + 32, hg, sb * 512 + soff:sb * 512 + soff + nw],
                start=True, stop=True, tile_position=(bp, 0))),
                reads=(("k", hg, t // 4), ("q", hg, sb)),
                writes=(UBR[hl],), tag="Umm")

    def emit_scores(sb, hg, t):
        sclo = max(t, 4 * sb)
        soff = (sclo - 4 * sb) * 128
        nw = (4 * sb + 4 - sclo) * 128
        diag = t >= 4 * sb
        s0, w0 = (soff + 128, nw - 128) if diag else (soff, nw)
        for p2 in range(2):
            ups = psB if p2 == 0 else psC
            if w0 > 0:
                bd.emit("act", (lambda e, p2=p2, s0=s0, w0=w0, ups=ups:
                                e.activation(
                    Asb[:, 2 * p2:2 * p2 + 2, s0:s0 + w0],
                    ups[:, :, s0:s0 + w0], AF.Square, bias=4.0)),
                    reads=(UBR[2 * p2], UBR[2 * p2 + 1], ("aux",)),
                    writes=(("Asb", 2 * p2), ("Asb", 2 * p2 + 1)),
                    tag="score", cols=2 * w0)
        if diag:
            jj = soff
            for p2 in range(2):
                ups = psB if p2 == 0 else psC
                bd.emit("act", (lambda e, p2=p2, jj=jj, ups=ups: e.activation(
                    diag_t[:, 2 * p2:2 * p2 + 2, :],
                    ups[:, :, jj:jj + 128], AF.Square, bias=4.0)),
                    reads=(UBR[2 * p2], UBR[2 * p2 + 1], ("aux",)),
                    writes=(("diag_t", p2),), tag="dsq", cols=256)
            for hl in range(4):
                bd.emit("dve", (lambda e, hl=hl, jj=jj: e.scalar_tensor_tensor(
                    Asb[:, hl, jj:jj + 128], diag_t[:, hl, :], 16.0, trif,
                    op0=AL.add, op1=AL.mult)),
                    reads=(("diag_t", hl // 2), ("trif",), ("Asb", hl)),
                    writes=(("Asb", hl),), tag="diag", cols=128)

    def emit_av(sb, hg, t):
        sclo = max(t, 4 * sb)
        soff = (sclo - 4 * sb) * 128
        nw = (4 * sb + 4 - sclo) * 128
        for hl in range(4):
            h = 4 * hg + hl
            bd.emit("pe", (lambda t_, h=h, hl=hl, tt=t, soff=soff, nw=nw:
                           t_.matmul(
                psA[0:65, hl, soff:soff + nw], Vp[:, tt, h, :],
                Asb[:, hl, soff:soff + nw], start=(tt == 0), stop=False)),
                reads=(("Vp", t), ("Asb", hl)), writes=(RA(hl),), tag="AVmm")

    for sb in range(NSB):
        for j in range(4):
            tc = 4 * sb + j
            bd.emit("dve", (lambda e, j=j, p=cpar[0]: e.tensor_copy(
                cumb[0:1, j, :], cumc[0:1, p, :])),
                reads=(("cumc",),), writes=(("cumb", j),), tag="cum_sn",
                cols=1040)
            segs = ((0, 512, psB[0:1, 0, 0:512], RB(0)),
                    (512, 512, psB[0:1, 1, 0:512], RB(1)),
                    (1024, 16, psC[0:1, 0, 0:16], RC(0)))
            vflat = Vp.rearrange("p t h v -> p t (h v)")
            for (o, w, ps, reg) in segs:
                bd.emit("pe", (lambda t, tc=tc, o=o, w=w, ps=ps: t.matmul(
                    ps, ONESCOL, vflat[:, tc, o:o + w], start=True, stop=True)),
                    reads=(("Vp", tc), ("trib",)), writes=(reg,), tag="colsum")
            p0, p1 = cpar[0], 1 - cpar[0]
            for (o, w, ps, reg) in segs:
                bd.emit("dve", (lambda e, o=o, w=w, ps=ps, p0=p0, p1=p1:
                                e.scalar_tensor_tensor(
                    cumc[0:1, p1, o:o + w], ps, 16.0, cumc[0:1, p0, o:o + w],
                    op0=AL.mult, op1=AL.add)),
                    reads=(reg, ("cumc",)), writes=(("cumc",),), tag="cum_up",
                    cols=w)
            cpar[0] = p1
        NTK = 4 * sb + 4
        for hg in range(4):
            emit_strips(sb, hg, 0)
            emit_scores(sb, hg, 0)
            for t in range(NTK):
                if t + 1 < NTK:
                    emit_strips(sb, hg, t + 1)
                emit_av(sb, hg, t)
                if t + 1 < NTK:
                    emit_scores(sb, hg, t + 1)
            for j in range(4):
                if sb == 0 and j == 0:
                    continue
                jj = j * 128
                for hl in range(4):
                    h = 4 * hg + hl
                    bd.emit("pe", (lambda t_, h=h, hl=hl, j=j, jj=jj:
                                   t_.matmul(
                        psA[0:65, hl, jj:jj + 128],
                        cumb[0:1, j, h * 65:(h + 1) * 65], ONESROW(),
                        start=False, stop=(j == 3))),
                        reads=(("cumb", j), ("trib",)), writes=(RA(hl),),
                        tag="fix")
            for hl in range(4):
                h = 4 * hg + hl
                bd.emit("act", (lambda e, h=h, hl=hl: e.activation(
                    y_st[(h % 2) * 64:(h % 2) * 64 + 64, h // 2, :],
                    psA[0:64, hl, :], AF.Copy)),
                    reads=(RA(hl),), writes=(("y_st", h),), tag="ynum",
                    cols=512)
            for hl in range(4):
                bd.emit("act", (lambda e, hl=hl: e.activation(
                    dvrf[0:1, hl, :], psA[64:65, hl, :], AF.Ln, bias=1e-9)),
                    reads=(RA(hl), ("aux",)), writes=(("dvrf", hl),),
                    tag="rln", cols=512)
            for hl in range(4):
                bd.emit("act", (lambda e, hl=hl: e.activation(
                    dvrb[0:1, hl, :], dvrf[0:1, hl, :], AF.Exp, scale=-1.0)),
                    reads=(("dvrf", hl), ("aux",)), writes=(("dvrb", hl),),
                    tag="rexp", cols=512)
            for hl in range(4):
                bd.emit("pe", (lambda t_, hl=hl: t_.matmul(
                    psA[0:64, hl, :], ONESROW(64), dvrb[0:1, hl, :],
                    start=True, stop=True)),
                    reads=(("dvrb", hl), ("trib",)), writes=(RA(hl),),
                    tag="rbc")
            for hl in range(4):
                h = 4 * hg + hl
                bd.emit("dve", (lambda e, h=h, hl=hl: e.tensor_mul(
                    y_st[(h % 2) * 64:(h % 2) * 64 + 64, h // 2, :],
                    y_st[(h % 2) * 64:(h % 2) * 64 + 64, h // 2, :],
                    psA[0:64, hl, :])),
                    reads=(("y_st", h), RA(hl)), writes=(("y_st", h),),
                    tag="ymul", cols=512)
        for m in range(CH):
            pb = m % 2
            for k in range(CH):
                bd.emit("pe", (lambda t_, m=m, k=k, pb=pb: t_.matmul(
                    psB[:, pb, :], wres[:, k, m * 128:(m + 1) * 128],
                    y_st[:, k, :], start=(k == 0), stop=(k == 7))),
                    reads=(("wres", k // 2), ("y_st", 2 * k), ("y_st", 2 * k + 1)),
                    writes=(RB(pb),), tag="wo_mm")
            bd.emit("dve", (lambda e, m=m, sb=sb, pb=pb: e.scalar_tensor_tensor(
                xs[:, m, sb * 512:(sb + 1) * 512], psB[:, pb, :],
                modv[:, 2, m:m + 1], xs[:, m, sb * 512:(sb + 1) * 512],
                op0=AL.mult, op1=AL.add)),
                reads=(RB(pb), ("modv",), ("xs", m, sb)),
                writes=(("xs", m, sb),), tag="xatt", cols=512)

    _stop = _os.environ.get("KSTOP", "")
    if _stop == "attn":
        for ch in range(CH):
            bd.emit("sp", (lambda e, ch=ch: e.dma_start(
                out=x_out[ch * 128:(ch + 1) * 128, :], in_=xs[:, ch, :])),
                reads=tuple(("xs", ch, nt) for nt in range(NSB)), writes=(),
                dma=True, tag=f"st_x{ch}")
        bd.finalize()
        _emit_all(nc, bd, sem)
        return nc
    bd.barrier_into([(n,) for n in LNSCR] + [("xbuf", i) for i in range(2)]
                    + [("xbq", i) for i in range(2)]
                    + [("tmpa", i) for i in range(2)]
                    + [("stat", 0), ("stat", 1)],
                    ("Vp",))
    bd.barrier_into(HM, P4SCR)
    layer_norm(3, 4, "ln2_")

    bd.barrier_into([("sel",), ("g_s",), ("comb",),
                     ("cbc", 0), ("cbc", 1), ("fbuf", 0), ("fbuf", 1)]
                    + [("tk", i) for i in range(4)]
                    + [("s_in", m) for m in range(CH)],
                    LNSCR + ("xbq",))
    bd.barrier_into([("hg", ei, hc) for ei in range(EGRP) for hc in range(HCN)],
                    ("q", "k"))
    dma_sp(sel, sel_in, (("sel",),), "ld_sel")
    for nt in range(NSB):
        for k in range(CH):
            bd.emit("pe", (lambda t_, nt=nt, k=k: t_.matmul(
                psC[0:16, 0, :], gatesb[:, k, :],
                hmod[:, k, nt * 512:(nt + 1) * 512],
                start=(k == 0), stop=(k == 7))),
                reads=(("gatesb",), ("hmod", k, nt)), writes=(RC(0),),
                tag="gate_mm")
        bd.emit("act", (lambda e, nt=nt: e.activation(
            g_s[:, nt * 512:(nt + 1) * 512], psC[0:16, 0, :], AF.Sigmoid)),
            reads=(RC(0),), writes=(("g_s",),), tag="gate_sig", cols=512)
    for g4 in range(NTC // 4):
        tcs = [4 * g4 + i for i in range(4)]
        for i, tc in enumerate(tcs):
            bd.emit("pe", (lambda t_, tc=tc, i=i: t_.matmul(
                psC[:, 1, 32 * i:32 * i + 16],
                g_s[:, tc * 128:(tc + 1) * 128], id16f,
                start=True, stop=True)),
                reads=(("g_s",), ("id16f",)), writes=(RC(1),), tag="gT")
        for i in range(4):
            bd.emit("dve", (lambda e, i=i: e.tensor_copy(
                gt4[:, i, :], psC[:, 1, 32 * i:32 * i + 16])),
                reads=(RC(1),), writes=(("tk", i),), tag="gT_ev", cols=16)
        for i in range(4):
            bd.emit("dve", (lambda e, i=i: e.max(out=m84[:, i, :],
                                                 in_=gt4[:, i, :])),
                    reads=(("tk", i),), writes=(("tk", i),), tag="max8",
                    cols=8)
        for i in range(4):
            bd.emit("dve", (lambda e, i=i: e.memset(m84[:, i, 4:8], 0.0)),
                    reads=(), writes=(("tk", i),), tag="m8z", cols=4)
        for i in range(4):
            bd.emit("dve", (lambda e, i=i: e.match_replace(
                out=zap4[:, i, :], in_to_replace=m84[:, i, :],
                in_values=gt4[:, i, :], imm_value=0.0)),
                reads=(("tk", i),), writes=(("tk", i),), tag="zap", cols=16)
        for i in range(4):
            bd.emit("dve", (lambda e, i=i: e.tensor_sub(
                mg4[:, i, :], gt4[:, i, :], zap4[:, i, :])),
                reads=(("tk", i),), writes=(("tk", i),), tag="mg", cols=16)
        for i in range(4):
            bd.emit("dve", (lambda e, i=i: e.reduce_sum(
                s44[:, i, 0:1], mg4[:, i, :], axis=mybir.AxisListType.X)),
                reads=(("tk", i),), writes=(("tk", i),), tag="s4", cols=16)
        for i in range(4):
            bd.emit("dve", (lambda e, i=i: e.reciprocal(
                s44[:, i, 1:2], s44[:, i, 0:1])),
                reads=(("tk", i),), writes=(("tk", i),), tag="rs4", cols=1)
        for i in range(4):
            bd.emit("dve", (lambda e, i=i: e.tensor_scalar(
                ctm4[:, i, :], mg4[:, i, :], s44[:, i, 1:2], None,
                op0=AL.mult)),
                reads=(("tk", i),), writes=(("tk", i),), tag="ctm", cols=16)
        for i in range(4):
            bd.emit("pe", (lambda t_, i=i: t_.matmul(
                psC[0:16, 1, 128 * i:128 * i + 128], ctm4[:, i, :], idb,
                start=True, stop=True)),
                reads=(("tk", i), ("idb",)), writes=(RC(1),), tag="cT")
        for i, tc in enumerate(tcs):
            bd.emit("act", (lambda e, tc=tc, i=i: e.activation(
                comb[:, tc * 128:(tc + 1) * 128],
                psC[0:16, 1, 128 * i:128 * i + 128], AF.Copy)),
                reads=(RC(1),), writes=(("comb",),), tag="cT_ev", cols=128)

    for nt in range(NSB):
        for g in range(NEG):
            for ei in range(EGRP):
                e_ = g * EGRP + ei
                cb_i = ei % 2
                bd.emit("pe", (lambda t_, e_=e_, nt=nt: t_.matmul(
                    psC[:, 1, 0:512], sel[:, e_ * 128:(e_ + 1) * 128],
                    comb[:, nt * 512:(nt + 1) * 512], start=True, stop=True)),
                    reads=(("sel",), ("comb",)), writes=(RC(1),), tag="cbc_mm")
                bd.emit("act", (lambda e, cb_i=cb_i: e.activation(
                    cbc[:, cb_i, :], psC[:, 1, 0:512], AF.Copy)),
                    reads=(RC(1),), writes=(("cbc", cb_i),), tag="cbc_ev", cols=512)
                for hc in range(HCN):
                    ph = 2 * (hc % 2)
                    for k in range(CH):
                        chk, rr = wchunk(128)
                        bd.emit("pe", (lambda t_, c=chk, k=k, nt=nt, ph=ph:
                                       t_.matmul(
                            psA[:, ph, :], c, hmod[:, k, nt * 512:(nt + 1) * 512],
                            start=(k == 0), stop=(k == 7))),
                            reads=(rr, ("hmod", k, nt)), writes=(RA(ph),),
                            tag="hin")
                    for k in range(CH):
                        chk, rr = wchunk(128)
                        bd.emit("pe", (lambda t_, c=chk, k=k, nt=nt, ph=ph:
                                       t_.matmul(
                            psA[:, ph + 1, :], c,
                            hmod[:, k, nt * 512:(nt + 1) * 512],
                            start=(k == 0), stop=(k == 7))),
                            reads=(rr, ("hmod", k, nt)), writes=(RA(ph + 1),),
                            tag="vin")
                    if _os.environ.get("KDBG_HIN") and e_ == 8 and nt == 0 and hc == 0:
                        bd.emit("dve", (lambda e, ph=ph: e.tensor_copy(
                            dbgs[:, 0, :], psA[:, ph, :])),
                            reads=(RA(ph),), writes=(("dbgs",),), tag="dbg0")
                        bd.emit("dve", (lambda e, ph=ph: e.tensor_copy(
                            dbgs[:, 1, :], psA[:, ph + 1, :])),
                            reads=(RA(ph + 1),), writes=(("dbgs",),), tag="dbg1")
                    fb = fbuf[:, 2 * (hc % 2), :]
                    fb2 = fbuf[:, 2 * (hc % 2) + 1, :]
                    bd.emit("act", (lambda e, ph=ph, fb=fb: e.activation(
                        fb, psA[:, ph, :], AF.Silu)),
                        reads=(RA(ph),), writes=(("fbuf", hc % 2),), tag="silu", cols=512)
                    bd.emit("dve", (lambda e, ph=ph, fb=fb, fb2=fb2:
                                    e.tensor_mul(fb2, fb, psA[:, ph + 1, :])),
                        reads=(("fbuf", hc % 2), RA(ph + 1)),
                        writes=(("fbuf", hc % 2),), tag="hv", cols=512)
                    bd.emit("dve", (lambda e, ei=ei, hc=hc, cb_i=cb_i, fb2=fb2:
                                    e.tensor_mul(
                        h_grp[:, ei, hc, :], fb2, cbc[:, cb_i, :])),
                        reads=(("fbuf", hc % 2), ("cbc", cb_i)),
                        writes=(("hg", ei, hc),), tag="hcb", cols=512)
            if g == 0:
                for m in range(CH):
                    ph = 2 * (m % 2)
                    for k in range(CH):
                        chk, rr = wchunk(128)
                        bd.emit("pe", (lambda t_, c=chk, k=k, nt=nt, ph=ph:
                                       t_.matmul(
                            psA[:, ph, :], c, hmod[:, k, nt * 512:(nt + 1) * 512],
                            start=(k == 0), stop=(k == 7))),
                            reads=(rr, ("hmod", k, nt)), writes=(RA(ph),),
                            tag="ws_mm")
                    for k in range(CH):
                        chk, rr = wchunk(128)
                        bd.emit("pe", (lambda t_, c=chk, k=k, nt=nt, ph=ph:
                                       t_.matmul(
                            psA[:, ph + 1, :], c,
                            hmod[:, k, nt * 512:(nt + 1) * 512],
                            start=(k == 0), stop=(k == 7))),
                            reads=(rr, ("hmod", k, nt)), writes=(RA(ph + 1),),
                            tag="vs_mm")
                    fb = fbuf[:, 2 * (m % 2), :]
                    bd.emit("act", (lambda e, ph=ph, fb=fb: e.activation(
                        fb, psA[:, ph, :], AF.Silu)),
                        reads=(RA(ph),), writes=(("fbuf", m % 2),), tag="s_silu", cols=512)
                    bd.emit("dve", (lambda e, m=m, ph=ph, fb=fb: e.tensor_mul(
                        s_in[:, m, :], fb, psA[:, ph + 1, :])),
                        reads=(("fbuf", m % 2), RA(ph + 1)),
                        writes=(("s_in", m),), tag="s_mul", cols=512)
            for m in range(CH):
                pb = m % 2
                if g == 0:
                    for k in range(CH):
                        chk, rr = wchunk(128)
                        bd.emit("pe", (lambda t_, c=chk, k=k, pb=pb: t_.matmul(
                            psB[:, pb, :], c, s_in[:, k, :],
                            start=(k == 0), stop=False)),
                            reads=(rr, ("s_in", k)), writes=(RB(pb),),
                            tag="wso_mm")
                for ei in range(EGRP):
                    for hc in range(HCN):
                        chk, rr = wchunk(128)
                        st2 = (g == 1 and ei == 0 and hc == 0)
                        bd.emit("pe", (lambda t_, c=chk, ei=ei, hc=hc, pb=pb,
                                        st2=st2: t_.matmul(
                            psB[:, pb, :], c, h_grp[:, ei, hc, :], start=st2,
                            stop=(ei == EGRP - 1 and hc == HCN - 1))),
                            reads=(rr, ("hg", ei, hc)), writes=(RB(pb),),
                            tag="wout_mm")
                bd.emit("dve", (lambda e, m=m, nt=nt, pb=pb:
                                e.scalar_tensor_tensor(
                    xs[:, m, nt * 512:(nt + 1) * 512], psB[:, pb, :],
                    modv[:, 5, m:m + 1], xs[:, m, nt * 512:(nt + 1) * 512],
                    op0=AL.mult, op1=AL.add)),
                    reads=(RB(pb), ("modv",), ("xs", m, nt)),
                    writes=(("xs", m, nt),), tag="xffn", cols=512)

    if _os.environ.get("KDBG_HIN"):
        bd.emit("sp", lambda e: e.dma_start(out=dbg_out, in_=dbgs),
                reads=(("dbgs",),), writes=(), dma=True, tag="st_dbg")
    else:
        bd.emit("dve", lambda e: e.memset(dbgs, 0.0),
                reads=(), writes=(("dbgs",),), tag="dbg_z")
        bd.emit("sp", lambda e: e.dma_start(out=dbg_out, in_=dbgs),
                reads=(("dbgs",),), writes=(), dma=True, tag="st_dbg")
    for ch in range(CH):
        bd.emit("sp", (lambda e, ch=ch: e.dma_start(
            out=x_out[ch * 128:(ch + 1) * 128, :], in_=xs[:, ch, :])),
            reads=tuple(("xs", ch, nt) for nt in range(NSB)), writes=(),
            dma=True, tag=f"st_x{ch}")

    assert st8["cursor"] == ffn0 + NSB * ffn_len, (st8["cursor"], ffn0, ffn_len)
    bd.finalize()
    _emit_all(nc, bd, sem)
    return nc


def _emit_all(nc, bd, sem):
    semmap = {"pe": sem["pe"], "act": sem["act"], "dve": sem["dve"],
              "sp": sem["sp"], "gp": sem["gp"],
              "gpdma": sem["gpdma"], "spdma": sem["spdma"]}

    def replay(engname, eng):
        wm = {}
        for op in bd.ops:
            if op["eng"] != engname:
                continue
            for key, val in sorted(op["waits"].items()):
                if wm.get(key, 0) >= val:
                    continue
                eng.wait_ge(semmap[key], val)
                wm[key] = val
            if op.get("drainb"):
                eng.drain()
            inst = op["fn"](eng)
            if op["dma"]:
                inst.then_inc(semmap[engname + "dma"], 16)
            elif op["awaited"]:
                inst.then_inc(semmap[engname], 1)
        if engname == "sp":
            eng.wait_ge(semmap["spdma"], 16 * bd.ndma["sp"])

    with nc.Block() as blk:
        @blk.tensor
        def _(t):
            replay("pe", t)

        @blk.scalar
        def _(s):
            replay("act", s)

        @blk.vector
        def _(v):
            replay("dve", v)

        @blk.gpsimd
        def _(g):
            replay("gp", g)

        @blk.sync
        def _(s):
            replay("sp", s)


def kernel(L0_attn_hidden_states, L0_attn_Wq, L0_attn_Wk, L0_attn_Wv,
           L0_attn_Wo, L0_ffn_gate_w, L0_ffn_ws_in, L0_ffn_vs_in,
           L0_ffn_ws_out, L0_ffn_e_win, L0_ffn_e_vin, L0_ffn_e_wout,
           L0_c, L0_wmod):
    S = L0_attn_hidden_states.shape[1]
    ws = dict(wmod=np.asarray(L0_wmod, np.float32),
              Wq=np.asarray(L0_attn_Wq, np.float32),
              Wk=np.asarray(L0_attn_Wk, np.float32),
              Wv=np.asarray(L0_attn_Wv, np.float32),
              Wo=np.asarray(L0_attn_Wo, np.float32),
              gate_w=np.asarray(L0_ffn_gate_w, np.float32),
              ws_in=np.asarray(L0_ffn_ws_in, np.float32),
              vs_in=np.asarray(L0_ffn_vs_in, np.float32),
              ws_out=np.asarray(L0_ffn_ws_out, np.float32),
              e_win=np.asarray(L0_ffn_e_win, np.float32),
              e_vin=np.asarray(L0_ffn_e_vin, np.float32),
              e_wout=np.asarray(L0_ffn_e_wout, np.float32))
    blob, offs, ffn0, ffn_len = build_wblob(ws)
    nc = build_program(S, blob.shape[1], offs, ffn0, ffn_len)

    x = np.asarray(L0_attn_hidden_states, np.float32)
    c = np.asarray(L0_c, np.float32)
    tri_f = np.triu(np.ones((128, 128), np.float32))
    tri_b = tri_f.astype(ml_dtypes.bfloat16)
    id_b = np.eye(128, dtype=ml_dtypes.bfloat16)
    id16 = np.eye(16, dtype=ml_dtypes.bfloat16)
    sel = np.zeros((16, E * 128), dtype=ml_dtypes.bfloat16)
    for e in range(E):
        sel[e, e * 128:(e + 1) * 128] = 1.0
    aux = np.zeros((128, 4), np.float32)
    aux[:, 0] = 1e-5
    aux[:, 1] = -1.0
    aux[:, 2] = 4.0
    aux[:, 3] = 1e-9

    n_cores = x.shape[0]
    in_maps = []
    for i in range(n_cores):
        in_maps.append({
            "x_in": np.ascontiguousarray(x[i].T),
            "wb": blob,
            "c_in": np.ascontiguousarray(c[i].reshape(CH, 128).T),
            "tri_f": tri_f, "tri_b": tri_b, "id_b": id_b, "id16_b": id16,
            "id16_f": np.eye(16, dtype=np.float32), "sel_b": sel, "aux_f": aux,
        })
    res = run_bass_kernel_spmd(nc, in_maps, core_ids=list(range(n_cores)))
    return np.stack([np.ascontiguousarray(r["x_out"].T) for r in res.results])
